# version 05
# baseline (speedup 1.0000x reference)
import contextlib
import numpy as np
import ml_dtypes

import concourse.bass as bass
import concourse.mybir as mybir
from concourse.bass_utils import run_bass_kernel_spmd

F32 = mybir.dt.float32
BF16 = mybir.dt.bfloat16
AF = mybir.ActivationFunctionType
ALU = mybir.AluOpType

D = 1024
T = 2048
DC = D // 128
TC = T // 128
NT = T // 512
NCORES = 8

GELU_C = 0.7978845608028654
GELU_C2 = GELU_C * 0.044715

BF = ml_dtypes.bfloat16



def chunk_w(w, din, dout):
    w = np.ascontiguousarray(np.asarray(w, dtype=np.float32).reshape(din, dout))
    kc, mc = din // 128, dout // 128
    a = w.reshape(kc, 128, mc, 128).transpose(2, 1, 0, 3)
    return np.ascontiguousarray(a.reshape(mc, 128, kc * 128))


def kslab_half(w, din, dout):
    w = np.ascontiguousarray(np.asarray(w, dtype=np.float32).reshape(din, dout))
    kc = din // 128
    hw = dout // 2
    a = w.reshape(kc, 128, 2, hw).transpose(2, 1, 0, 3)
    return np.ascontiguousarray(a.reshape(2, 128, kc * hw))


def pdim(v):
    v = np.asarray(v, dtype=np.float32).reshape(-1)
    return np.ascontiguousarray(v.reshape(-1, 128).T)


def row_repl(v, parts=128):
    v = np.asarray(v, dtype=np.float32).reshape(1, -1)
    return np.ascontiguousarray(np.broadcast_to(v, (parts, v.shape[1])))


def make_consts():
    c = {}
    c["ones_col_f32"] = np.ones((128, 1), np.float32)
    c["ones_col_bf"] = np.ones((128, 1), np.float32)
    c["ones_1x128"] = np.ones((1, 128), np.float32)
    sel65 = np.zeros((65, 128), np.float32)
    sel65[64, :] = 1.0
    c["sel65"] = sel65
    c["ident"] = np.eye(128, dtype=np.float32)
    c["tri"] = np.triu(np.ones((128, 128), np.float32))
    e8 = np.zeros((8, 8 * 128), np.float32)
    for e in range(8):
        e8[e, 128 * e:128 * (e + 1)] = 1.0
    c["e8sel"] = e8
    e16 = np.zeros((16, 16 * 128), np.float32)
    for e in range(16):
        e16[e, 128 * e:128 * (e + 1)] = 1.0
    c["e16sel"] = e16
    return c



ENGINES = ("tensor", "vector", "scalar", "sync", "gpsimd")


class Ctr:
    __slots__ = ("sem", "val")

    def __init__(self, sem):
        self.sem = sem
        self.val = 0


ARENA_BYTES = 204 * 1024


class Arena:

    def __init__(self, nc, ctx, size=ARENA_BYTES):
        self.nc = nc
        self.base = ctx.enter_context(
            nc.sbuf_tensor("arena", [128, size], mybir.dt.uint8))
        self.base_addr = nc.lookup_mloc(self.base).addr
        self.free_list = [(0, size)]
        self.ranges = {}

    def alloc(self, name, shape, dtype, from_end=False):
        per_part = 1
        for s in shape[1:]:
            per_part *= s
        per_part *= mybir.dt.size(dtype)
        need = (per_part + 31) // 32 * 32
        order = reversed(list(enumerate(self.free_list))) if from_end             else enumerate(self.free_list)
        for i, (off, ln) in enumerate(self.free_list) if not from_end else                 reversed(list(enumerate(self.free_list))):
            if ln >= need:
                if ln == need:
                    self.free_list.pop(i)
                elif from_end:
                    off = off + ln - need
                    self.free_list[i] = (self.free_list[i][0], ln - need)
                else:
                    self.free_list[i] = (off + need, ln - need)
                h = self.nc.virtual_sbuf_tensor(name, shape, dtype)
                self.nc._tensor(h.name, h.shape, h.dtype, type="SB")
                mloc = self.nc.lookup_mloc(h)
                mloc.allocated = True
                mloc.addr = self.base_addr + off
                self.ranges[h.name] = (off, need)
                return h
        live = sorted(self.ranges.items(), key=lambda kv: -kv[1][1])
        detail = ", ".join(f"{k}:{v[1]}" for k, v in live[:25])
        raise MemoryError(f"arena OOM allocating {name} {shape} ({need}B); "
                          f"free={self.free_list}; live: {detail}")

    def free(self, h):
        off, ln = self.ranges.pop(h.name)
        self.free_list.append((off, ln))
        self.free_list.sort()
        merged = []
        for o, l in self.free_list:
            if merged and merged[-1][0] + merged[-1][1] == o:
                merged[-1] = (merged[-1][0], merged[-1][1] + l)
            else:
                merged.append((o, l))
        self.free_list = [tuple(x) for x in merged]


class ArenaScope:
    def __init__(self, arena):
        self.arena = arena
        self.handles = []
        self.stacks = []

    def add(self, h):
        self.handles.append(h)

    def enter_context(self, cm):
        st = contextlib.ExitStack()
        r = st.enter_context(cm)
        self.stacks.append(st)
        return r

    def close(self):
        for h in self.handles:
            self.arena.free(h)
        self.handles = []
        for st in reversed(self.stacks):
            st.close()
        self.stacks = []


class Prog:
    def __init__(self, nc, ctx):
        self.nc = nc
        self.ctx = ctx
        self.ops = {e: [] for e in ENGINES}
        self._nm = 0
        self.ctrs = []
        self.ec = {e: self.new_ctr("c_" + e) for e in ("tensor", "vector", "scalar")}
        self.fence_waits = []
        self.arena = Arena(nc, ctx)
        self._global_scope = ArenaScope(self.arena)

    def name(self, base):
        self._nm += 1
        return f"{base}_{self._nm}"

    def new_ctr(self, base):
        sem = self.ctx.enter_context(self.nc.semaphore(self.name(base)))
        c = Ctr(sem)
        self.ctrs.append(c)
        return c

    def sbuf(self, base, shape, dtype, scope=None, from_end=False):
        h = self.arena.alloc(self.name(base), shape, dtype, from_end=from_end)
        (scope or self._global_scope).add(h)
        return h

    def psum(self, base, shape, dtype=F32, scope=None):
        sc = scope or self._global_scope
        return sc.enter_context(self.nc.psum_tensor(self.name(base), shape, dtype))

    def scope(self):
        return ArenaScope(self.arena)

    def emit(self, eng, fn, waits=(), inc=None, nofence=False):
        w = []
        for x in waits:
            if x is None:
                continue
            if isinstance(x, tuple) and len(x) == 2 and x[0] == "multi":
                w.extend(t for t in x[1] if t is not None)
            else:
                w.append(x)
        if not nofence and self.fence_waits:
            w = list(self.fence_waits) + w
        if eng in ("vector", "scalar"):
            inc = True if inc is None else inc
        if inc is False:
            inc = None
        tok = None
        if inc is True:
            c = self.ec[eng]
            c.val += 1
            tok = (c, c.val)
            inc = (c, 1)
        elif inc:
            c, n = inc
            c.val += n
            tok = (c, c.val)
        self.ops[eng].append((tuple(w), fn, inc))
        return tok

    def tok(self, eng):
        c = self.ec[eng]
        return (c, c.val)

    def new_phase(self):
        self.fence_waits = [(c, c.val) for c in self.ctrs if c.val > 0]

    def dump(self, name, ap, shape, dtype, waits=()):
        import os
        if not os.environ.get("KB_DEBUG"):
            return
        if not hasattr(self, "dbg_ctr"):
            self.dbg_ctr = self.new_ctr("dbg")
        dr = self.nc.dram_tensor("dbg_" + name, list(shape), dtype,
                                 kind="ExternalOutput").ap()
        self.emit("sync", lambda e, o=dr, i=ap: e.dma_start(out=o, in_=i),
                  waits=waits, inc=(self.dbg_ctr, 16))

    def emit_all(self, block):
        def runner(engname):
            def run(engobj):
                waited = {}
                owninc = {}
                for waits, fn, inc in self.ops[engname]:
                    for (c, v) in waits:
                        if waited.get(c, 0) >= v:
                            continue
                        if owninc.get(c, 0) >= v:
                            engobj.wait_ge(c.sem, v)
                            waited[c] = v
                        else:
                            engobj.wait_ge(c.sem, v)
                            waited[c] = v
                    ins = fn(engobj)
                    if inc is not None:
                        c, n = inc
                        ins.then_inc(c.sem, n)
                        owninc[c] = owninc.get(c, 0) + n
            return run
        block.tensor(runner("tensor"))
        block.vector(runner("vector"))
        block.scalar(runner("scalar"))
        block.sync(runner("sync"))
        block.gpsimd(runner("gpsimd"))


class Ring:
    def __init__(self, P, base, cols, dtype=BF16, depth=2, queue="gpsimd",
                 scope=None, parts=128):
        self.P = P
        self.slots = [P.sbuf(base, [parts, cols], dtype, scope=scope)
                      for _ in range(depth)]
        self.ctr = P.new_ctr(base + "_dma")
        self.depth = depth
        self.i = 0
        self.free_tok = [None] * depth
        self.queue = queue

    def load(self, src_ap, extra_waits=(), out_view=None):
        s = self.i % self.depth
        waits = list(extra_waits)
        first = self.free_tok[s] is None
        if not first:
            waits.append(self.free_tok[s])
        slot = self.slots[s]
        oap = slot.ap() if out_view is None else out_view(slot.ap())
        self.P.emit(self.queue,
                    lambda e, o=oap, i=src_ap: e.dma_start(out=o, in_=i),
                    waits=waits, inc=(self.ctr, 16), nofence=not first)
        self.i += 1
        return s, slot, (self.ctr, 16 * self.i)

    def free(self, s, tok):
        assert tok is not None, "ring slot freed without a consumer token"
        self.free_tok[s] = tok



def stats_rows(P, sc, pst, consts, eps, stats_done, rstd_scope=None):
    mu = P.sbuf("r_mu", [1, T], F32, scope=sc)
    s2 = P.sbuf("r_s2", [1, T], F32, scope=sc)
    tt = P.sbuf("r_t", [1, T], F32, scope=sc)
    rstd = P.sbuf("r_rstd", [1, T], F32, scope=rstd_scope or sc)
    t1 = P.emit("vector", lambda e: e.tensor_scalar(
        mu.ap(), pst.ap()[0:1, :], 1.0 / D, None, ALU.mult), waits=[stats_done])
    t2 = P.emit("vector", lambda e: e.tensor_scalar(
        s2.ap(), pst.ap()[32:33, :], 1.0 / D, None, ALU.mult), waits=[stats_done])
    t3 = P.emit("scalar", lambda e: e.activation(tt.ap(), mu.ap(), AF.Square),
                waits=[t1])
    t4 = P.emit("vector", lambda e: e.tensor_sub(s2.ap(), s2.ap(), tt.ap()),
                waits=[t2, t3])
    t4b = P.emit("vector", lambda e: e.tensor_scalar(
        tt.ap(), s2.ap(), float(eps), None, ALU.add), waits=[t4])
    t5 = P.emit("scalar", lambda e: e.activation(rstd.ap(), tt.ap(), AF.Ln),
                waits=[t4b])
    t6 = P.emit("scalar", lambda e: e.activation(rstd.ap(), rstd.ap(), AF.Exp,
                                                 scale=-0.5), waits=[t5])
    return dict(mu=mu, var=s2, rstd=rstd, tmp=tt,
                t_mu=t1, t_var=t4, t_rstd=t6, t_reads=t2)


def ln_stats(P, sc, src, consts, eps, src_tok=None, rstd_scope=None):
    pst = P.psum("pst", [128, T], F32, scope=sc)
    sq = [P.sbuf("sq", [128, T], BF16, scope=sc) for _ in range(2)]
    pe_tok = [None, None]
    tk = None
    for c in range(DC):
        s = c % 2
        t_sq = P.emit("scalar", lambda e, o=sq[s].ap(), i=src.ap()[:, c * T:(c + 1) * T]:
                      e.activation(o, i, AF.Square),
                      waits=[pe_tok[s], src_tok])
        for nt in range(NT):
            P.emit("tensor", lambda e, o=pst.ap()[0:1, nt * 512:(nt + 1) * 512],
                   l=consts["ones_col_f32"].ap(),
                   r=src.ap()[:, c * T + nt * 512: c * T + (nt + 1) * 512],
                   s_=(c == 0):
                   e.matmul(o, l, r, start=s_, stop=False, skip_group_check=True),
                   waits=[src_tok])
        for nt in range(NT):
            tk = P.emit("tensor", lambda e, o=pst.ap()[32:33, nt * 512:(nt + 1) * 512],
                        l=consts["ones_col_bf"].ap(),
                        r=sq[s].ap()[:, nt * 512:(nt + 1) * 512]:
                        e.matmul(o, l, r, start=False, stop=False, skip_group_check=True),
                        waits=[t_sq], inc=True if nt == NT - 1 else None)
        pe_tok[s] = tk
    rows = stats_rows(P, sc, pst, consts, eps, P.tok("tensor"), rstd_scope)
    rows["pst"] = pst
    return rows


def bcast_rows(P, pst, consts, items, prev_evac=None):
    out = []
    mm = None
    stage = P.sbuf("bcrow", [1, T], F32, scope=items[0][2])
    for (row, rtok, dsc) in items:
        tcp = P.emit("scalar", lambda e, o=stage.ap(), i=row.ap(): e.copy(o, i),
                     waits=[rtok, mm])
        for nt in range(NT):
            mm = P.emit("tensor", lambda e, o=pst.ap()[:, nt * 512:(nt + 1) * 512],
                        l=consts["ones_1x128"].ap(),
                        r=stage.ap()[:, nt * 512:(nt + 1) * 512]:
                        e.matmul(o, l, r, start=True, stop=True, skip_group_check=True),
                        waits=[tcp, prev_evac], inc=nt == NT - 1)
        tile = P.sbuf("bctile", [128, T], F32, scope=dsc)
        prev_evac = P.emit("scalar", lambda e, o=tile.ap(), i=pst.ap():
                           e.activation(o, i, AF.Copy), waits=[mm])
        out.append((tile, prev_evac))
    return out


def ln_apply(P, src, rbc, rbc_tok, mbc, mbc_tok, out, scale, bias, scr,
             src_tok=None, src_off=0):
    last = None
    eng_tok = None
    for c in range(DC):
        t1 = P.emit("vector", lambda e, o=scr.ap(),
                    i=src.ap()[:, src_off + c * T: src_off + (c + 1) * T],
                    r=rbc.ap(): e.tensor_mul(o, i, r),
                    waits=[rbc_tok, src_tok, eng_tok])
        if scale is None:
            last = P.emit("vector", lambda e, o=out.ap()[:, c * T:(c + 1) * T],
                          a=scr.ap(), b=mbc.ap(): e.tensor_sub(o, a, b),
                          waits=[mbc_tok, t1])
            eng_tok = last
        else:
            t2 = P.emit("vector", lambda e, o=scr.ap(), a=scr.ap(), b=mbc.ap():
                        e.tensor_sub(o, a, b), waits=[mbc_tok, t1])
            last = P.emit("scalar", lambda e, o=out.ap()[:, c * T:(c + 1) * T],
                          i=scr.ap(), sa=scale.ap()[:, c:c + 1],
                          ba=bias.ap()[:, c:c + 1]:
                          e.activation(o, i, AF.Identity, bias=ba, scale=sa), waits=[t2])
            eng_tok = last
    return last


def ws_matmul(P, wring, wsrc_fn, rhs_off_fn, psum_views, evac_fn, mc_range,
              kc, ntile, rhs_toks=()):
    npp = len(psum_views)
    evac_toks = [None] * npp
    for idx, mc in enumerate(mc_range):
        si, slot, ready = wring.load(wsrc_fn(mc))
        pv = psum_views[idx % npp]
        mm = None
        for k in range(kc):
            for nt in range(ntile):
                mm = P.emit("tensor",
                            lambda e, o=pv[nt],
                            l=slot.ap()[:, k * 128:(k + 1) * 128],
                            r=rhs_off_fn(k, nt), s_=(k == 0):
                            e.matmul(o, l, r, start=s_, stop=False,
                                     skip_group_check=True),
                            waits=list(rhs_toks) + [ready, evac_toks[idx % npp]],
                            inc=True if (k == kc - 1 and nt == ntile - 1) else None)
        wring.free(si, mm)
        evac_toks[idx % npp] = evac_fn(mc, mm)



def bld_l0attn(P, xT, consts, W, xres_dram, x_tok=None):
    sc0 = P.scope()
    sck = P.scope()
    rows = ln_stats(P, sc0, xT, consts, 1e-5, src_tok=x_tok, rstd_scope=sck)
    r_m0r0 = P.sbuf("r_m0r0", [1, T], F32, scope=sck)
    u1 = P.sbuf("r_u1", [1, T], F32, scope=sc0)
    u2 = P.sbuf("r_u2", [1, T], F32, scope=sc0)
    tv, tmu, trs0 = rows["t_var"], rows["t_mu"], rows["t_rstd"]
    t_m0r0 = P.emit("vector", lambda e: e.tensor_mul(
        r_m0r0.ap(), rows["mu"].ap(), rows["rstd"].ap()), waits=[tmu, trs0])
    t = P.emit("vector", lambda e: e.tensor_scalar(u1.ap(), rows["var"].ap(), 1e-5, None, ALU.add), waits=[tv])
    t = P.emit("vector", lambda e: e.reciprocal(u2.ap(), u1.ap()), waits=[t])
    t = P.emit("vector", lambda e: e.tensor_mul(u1.ap(), rows["var"].ap(), u2.ap()), waits=[t])
    t = P.emit("vector", lambda e: e.tensor_scalar(u2.ap(), u1.ap(), 1e-6, None, ALU.add), waits=[t])
    t = P.emit("scalar", lambda e: e.activation(u1.ap(), u2.ap(), AF.Ln), waits=[t])
    t = P.emit("scalar", lambda e: e.activation(u2.ap(), u1.ap(), AF.Exp, scale=-0.5), waits=[t])
    t_rs = P.emit("vector", lambda e: e.tensor_mul(u1.ap(), rows["rstd"].ap(), u2.ap()), waits=[t, trs0])
    t_mrs = P.emit("vector", lambda e: e.tensor_mul(u2.ap(), rows["mu"].ap(), u1.ap()), waits=[t_rs, tmu])
    bc = bcast_rows(P, rows["pst"], consts,
                    [(u1, t_rs, sc0), (u2, t_mrs, sc0)],
                    prev_evac=rows["t_reads"])
    (rs_bc, t_rsbc), (mrs_bc, t_mrsbc) = bc

    scb = P.scope()
    b_buf = P.sbuf("b_buf", [128, DC * T], BF16, scope=scb)
    scr1 = P.sbuf("scr1", [128, T], F32, scope=sc0)
    t_b = ln_apply(P, xT, rs_bc, t_rsbc, mrs_bc, t_mrsbc, b_buf,
                   W["ln1_s"], W["ln1_b"], scr1, src_tok=x_tok)
    sc0.close()
    P.new_phase()

    scq = P.scope()
    phiq = P.sbuf("phiq", [128, DC * T], BF16, scope=scq)
    sc2 = P.scope()
    scr2 = [P.sbuf("scr2", [128, T], F32, scope=sc2) for _ in range(2)]
    wq_ring = Ring(P, "wqr", DC * 128, depth=2, scope=sc2)
    pq = [P.psum("pq", [128, T], F32, scope=sc2) for _ in range(2)]

    def evac_q(mc, mm_tok):
        s = mc % 2
        t1 = P.emit("vector", lambda e, o=scr2[s].ap(), i=pq[s].ap(),
                    b_=W["bq"].ap()[:, mc:mc + 1]:
                    e.tensor_scalar(o, i, b_, 0.0, ALU.add, ALU.max), waits=[mm_tok])
        return P.emit("vector", lambda e, o=phiq.ap()[:, mc * T:(mc + 1) * T],
                      i=scr2[s].ap(): e.tensor_scalar(o, i, 1e-3, None, ALU.add),
                      waits=[t1])

    ws_matmul(P, wq_ring, lambda mc: W["Wq"].ap()[mc],
              lambda k, nt: b_buf.ap()[:, k * T + nt * 512: k * T + (nt + 1) * 512],
              [[pq[0].ap()[:, nt * 512:(nt + 1) * 512] for nt in range(NT)],
               [pq[1].ap()[:, nt * 512:(nt + 1) * 512] for nt in range(NT)]],
              evac_q, range(DC), DC, NT, rhs_toks=[t_b])
    t_phiq = P.tok("vector")
    sc2.close()
    P.new_phase()

    sckv = P.scope()
    kv64 = P.sbuf("kv64", [128, 2048], BF16, scope=sckv)
    sc3 = P.scope()
    wk_ring = Ring(P, "wk", DC * 512, depth=2, scope=sc3)
    wv_ring = Ring(P, "wv", DC * 512, depth=2, scope=sc3)
    vt = P.sbuf("vt", [128, 8, 65], BF16, scope=sc3)
    pk_sb = [P.sbuf("pk_sb", [128, 512], BF16, scope=sc3) for _ in range(2)]
    scr3 = [P.sbuf("scr3", [128, 512], F32, scope=sc3) for _ in range(2)]
    p_k = P.psum("p_k", [128, 512], F32, scope=sc3)
    p_v = P.psum("p_v", [128, 512], F32, scope=sc3)
    kv_ps = P.psum("kv_ps", [128, 2048], F32, scope=sc3)

    t_ones = P.emit("vector", lambda e: e.memset(vt.ap()[:, :, 64:65], 1.0))
    pa_tok = None
    wk_free = wv_free = None
    for half in range(2):
        _, wk_slot, wk_ready = wk_ring.load(W["Wk"].ap()[half],
                                            extra_waits=[wk_free])
        _, wv_slot, wv_ready = wv_ring.load(W["Wv"].ap()[half],
                                            extra_waits=[wv_free])
        t1_last = None
        ev_v = None
        pa_hist = [None, None]
        for tc in range(TC):
            s = tc % 2
            mmk = None
            for k in range(DC):
                mmk = P.emit("tensor", lambda e, o=p_k.ap(),
                             l=b_buf.ap()[:, k * T + tc * 128: k * T + tc * 128 + 128],
                             r=wk_slot.ap()[:, k * 512:(k + 1) * 512],
                             s_=(k == 0):
                             e.matmul(o, l, r, start=s_, stop=False, skip_group_check=True),
                             waits=[wk_ready, t_b, t1_last],
                             inc=True if k == DC - 1 else None)
            t1 = P.emit("vector", lambda e, o=scr3[s].ap(), i=p_k.ap(),
                        b_=W["bk_bc"].ap()[:, half * 512:(half + 1) * 512]:
                        e.scalar_tensor_tensor(o, i, 0.0, b_, ALU.add, ALU.add),
                        waits=[mmk])
            t1_last = t1
            tk2 = P.emit("vector", lambda e, o=pk_sb[s].ap(), i=scr3[s].ap():
                         e.tensor_scalar(o, i, 0.0, 1e-3, ALU.max, ALU.add),
                         waits=[t1, pa_hist[s]])
            mmv = None
            for k in range(DC):
                mmv = P.emit("tensor", lambda e, o=p_v.ap(),
                             l=b_buf.ap()[:, k * T + tc * 128: k * T + tc * 128 + 128],
                             r=wv_slot.ap()[:, k * 512:(k + 1) * 512],
                             s_=(k == 0):
                             e.matmul(o, l, r, start=s_, stop=False, skip_group_check=True),
                             waits=[wv_ready, t_b, ev_v],
                             inc=True if k == DC - 1 else None)
            ev_v = P.emit("vector", lambda e, o=vt.ap()[:, :, 0:64],
                          i=p_v.ap().rearrange("p (h m) -> p h m", h=8),
                          b_=W["bv_bc"].ap()[:, half * 512:(half + 1) * 512].rearrange(
                              "p (h m) -> p h m", h=8):
                          e.scalar_tensor_tensor(o, i, 0.0, b_, ALU.add, ALU.add),
                          waits=[mmv, t_ones, pa_tok])
            for hh in range(8):
                h = half * 8 + hh
                pa_tok = P.emit("tensor", lambda e,
                                o=kv_ps.ap()[0:64, 128 * h:128 * h + 65],
                                l=pk_sb[s].ap()[:, 64 * hh:64 * hh + 64],
                                r=vt.ap()[:, hh, :],
                                s_=(tc == 0 and h % 4 == 0):
                                e.matmul(o, l, r, start=s_, stop=False, skip_group_check=True),
                                waits=[tk2, ev_v], inc=True if hh == 7 else None)
            pa_hist[s] = pa_tok
        wk_free = wv_free = pa_tok

    t_kv = P.emit("vector", lambda e: e.tensor_copy(kv64.ap()[0:64, :], kv_ps.ap()[0:64, :]),
                  waits=[pa_tok])
    mv_ctr = P.new_ctr("kvmv")
    P.emit("sync", lambda e: e.dma_start(
        out=kv64.ap()[64:128, :].rearrange("p (h c) -> p h c", h=8)[:, :, 128:256],
        in_=kv64.ap()[0:64, :].rearrange("p (h c) -> p h c", h=8)[:, :, 128:256]),
        waits=[t_kv], inc=(mv_ctr, 16))
    t_kvmv = (mv_ctr, 16)
    sc3.close()
    scb.close()
    P.new_phase()

    sca = P.scope()
    attn = P.sbuf("attn", [128, DC * T], BF16, scope=sca)
    sc4 = P.scope()
    recip_buf = P.sbuf("recipb", [65, T], F32, scope=sc4)
    rbc_sb = P.sbuf("rbc_sb", [128, T], F32, scope=sc4)
    pnd = P.psum("pnd", [128, 2048], F32, scope=sc4)
    pbc = P.psum("pbc", [128, 2048], F32, scope=sc4)
    t_z = P.emit("vector", lambda e: e.memset(recip_buf.ap(), 0.0))

    prev_mult = None
    prev_bcev = None
    for h in range(16):
        p64 = 64 * (h % 2)
        mm = None
        for nt in range(NT):
            mm = P.emit("tensor", lambda e,
                        o=pnd.ap()[0:65, nt * 512:(nt + 1) * 512],
                        l=kv64.ap()[p64:p64 + 64, 128 * h:128 * h + 65],
                        r=phiq.ap()[p64:p64 + 64, (h // 2) * T + nt * 512:(h // 2) * T + (nt + 1) * 512]:
                        e.matmul(o, l, r, start=True, stop=True, skip_group_check=True),
                        waits=[t_kvmv, t_phiq, prev_mult], inc=nt == NT - 1)
        t_r = P.emit("vector", lambda e: e.reciprocal(
            recip_buf.ap()[64:65, :], pnd.ap()[64:65, :]), waits=[mm, t_z])
        bcmm = None
        for nt in range(NT):
            bcmm = P.emit("tensor", lambda e,
                          o=pbc.ap()[:, nt * 512:(nt + 1) * 512],
                          l=consts["sel65"].ap(),
                          r=recip_buf.ap()[0:65, nt * 512:(nt + 1) * 512]:
                          e.matmul(o, l, r, start=True, stop=True, skip_group_check=True),
                          waits=[t_r, prev_bcev], inc=nt == NT - 1)
        prev_bcev = P.emit("scalar", lambda e: e.activation(
            rbc_sb.ap(), pbc.ap(), AF.Copy), waits=[bcmm])
        if h % 2 == 0:
            out_ap = attn.ap()[0:64, (h // 2) * T:(h // 2 + 1) * T]
        else:
            out_ap = phiq.ap()[0:64, (h // 2) * T:(h // 2 + 1) * T]
        prev_mult = P.emit("vector", lambda e, o=out_ap:
                           e.tensor_mul(o, pnd.ap()[0:64, :], rbc_sb.ap()[0:64, :]),
                           waits=[prev_bcev])
    mv2 = P.new_ctr("oddmv")
    P.emit("sync", lambda e: e.dma_start(out=attn.ap()[64:128, :],
                                         in_=phiq.ap()[0:64, :]),
           waits=[prev_mult], inc=(mv2, 16))
    t_attn = (mv2, 16)
    sc4.close()
    scq.close()
    sckv.close()
    P.new_phase()

    sc5 = P.scope()
    scb5 = P.scope()
    sc5a = P.scope()
    pbc5 = P.psum("pbc5", [128, T], F32, scope=scb5)
    bcx = bcast_rows(P, pbc5, consts,
                     [(rows["rstd"], None, sc5a), (r_m0r0, None, sc5a)],
                     prev_evac=None)
    (r0_bc, t_r0bc), (m0_bc, t_m0bc) = bcx
    scb5.close()
    sck.close()
    pst2 = P.psum("pst2", [128, T], F32, scope=sc5)
    pwo = P.psum("pwo", [128, T], F32, scope=sc5)
    wo_ring = Ring(P, "wor", DC * 128, depth=3, scope=sc5a)
    chunkb = [P.sbuf("chunkb", [128, T], F32, scope=sc5a) for _ in range(2)]
    sq5 = [P.sbuf("sq5", [128, T], BF16, scope=sc5a) for _ in range(2)]
    scr5 = P.sbuf("scr5", [128, T], F32, scope=sc5a)
    xrd3 = xres_dram.rearrange("(c p) t -> p c t", p=128)
    sp_ctr = P.new_ctr("xspill")
    slot_busy = [None, None]

    def evac_wo(mc, mm_tok):
        s = mc % 2
        t1 = P.emit("vector", lambda e, o=scr5.ap(),
                    i=xT.ap()[:, mc * T:(mc + 1) * T], r=r0_bc.ap():
                    e.tensor_mul(o, i, r), waits=[t_r0bc, mm_tok, x_tok])
        t2 = P.emit("vector", lambda e, o=chunkb[s].ap(), i=pwo.ap(),
                    b_=W["bo"].ap()[:, mc:mc + 1], r=scr5.ap():
                    e.scalar_tensor_tensor(o, i, b_, r, ALU.add, ALU.add),
                    waits=[t1, slot_busy[s]])
        t3 = P.emit("vector", lambda e, o=chunkb[s].ap(), a=chunkb[s].ap(),
                    b=m0_bc.ap(): e.tensor_sub(o, a, b), waits=[t2, t_m0bc])
        tsq = P.emit("scalar", lambda e, o=sq5[s].ap(), i=chunkb[s].ap():
                     e.activation(o, i, AF.Square), waits=[t3])
        for nt in range(NT):
            P.emit("tensor", lambda e, o=pst2.ap()[0:1, nt * 512:(nt + 1) * 512],
                   l=consts["ones_col_f32"].ap(),
                   r=chunkb[s].ap()[:, nt * 512:(nt + 1) * 512], s_=(mc == 0):
                   e.matmul(o, l, r, start=s_, stop=False, skip_group_check=True),
                   waits=[t3])
        pe_t = None
        for nt in range(NT):
            pe_t = P.emit("tensor", lambda e, o=pst2.ap()[32:33, nt * 512:(nt + 1) * 512],
                          l=consts["ones_col_bf"].ap(),
                          r=sq5[s].ap()[:, nt * 512:(nt + 1) * 512]:
                          e.matmul(o, l, r, start=False, stop=False, skip_group_check=True),
                          waits=[tsq], inc=True if nt == NT - 1 else None)
        sp = P.emit("sync", lambda e, o=xrd3[:, mc, :],
                    i=chunkb[s].ap(): e.dma_start(out=o, in_=i),
                    waits=[t3], inc=(sp_ctr, 16))
        t4 = P.emit("vector", lambda e, o=xT.ap()[:, mc * T:(mc + 1) * T],
                    a=xT.ap()[:, mc * T:(mc + 1) * T], b=chunkb[s].ap():
                    e.tensor_add(o, a, b), waits=[t3])
        slot_busy[s] = ("multi", [t4, sp, pe_t])
        return t4

    ws_matmul(P, wo_ring, lambda mc: W["Wo"].ap()[mc],
              lambda k, nt: attn.ap()[:, k * T + nt * 512: k * T + (nt + 1) * 512],
              [[pwo.ap()[:, nt * 512:(nt + 1) * 512] for nt in range(NT)]],
              evac_wo, range(DC), DC, NT, rhs_toks=[t_attn])
    t_spill_all = (sp_ctr, 16 * DC)
    t_add_all = P.tok("vector")
    stats2_done = P.tok("tensor")
    sc5a.close()
    sca.close()

    rows2 = stats_rows(P, sc5, pst2, consts, 1e-6, stats2_done)
    r_m2 = P.sbuf("r_m2", [1, T], F32, scope=sc5)
    t_m2 = P.emit("vector", lambda e: e.tensor_mul(
        r_m2.ap(), rows2["mu"].ap(), rows2["rstd"].ap()),
        waits=[rows2["t_mu"], rows2["t_rstd"]])
    bc6 = bcast_rows(P, pst2, consts,
                     [(rows2["rstd"], rows2["t_rstd"], sc5), (r_m2, t_m2, sc5)],
                     prev_evac=rows2["t_reads"])
    (r2_bc, t_r2bc), (m2_bc, t_m2bc) = bc6
    P.new_phase()

    scxn = P.scope()
    xn2 = P.sbuf("xn2", [128, DC * T], BF16, scope=scxn)
    sc6 = P.scope()
    rl_ctr = P.new_ctr("xrl")
    rl = [P.sbuf("rl", [128, T], F32, scope=sc6) for _ in range(2)]
    scr6 = P.sbuf("scr6", [128, T], F32, scope=sc6)
    rd_busy = [None, None]
    t_xn2 = None
    eng_tok = None
    for c in range(DC):
        s = c % 2
        P.emit("sync", lambda e, o=rl[s].ap(), i=xrd3[:, c, :]:
               e.dma_start(out=o, in_=i),
               waits=[t_spill_all, rd_busy[s]], inc=(rl_ctr, 16))
        t_rl = (rl_ctr, 16 * (c + 1))
        t1 = P.emit("vector", lambda e, o=scr6.ap(), i=rl[s].ap(), r=r2_bc.ap():
                    e.tensor_mul(o, i, r), waits=[t_rl, t_r2bc, eng_tok])
        t2 = P.emit("vector", lambda e, o=scr6.ap(), a=scr6.ap(), b=m2_bc.ap():
                    e.tensor_sub(o, a, b), waits=[t1, t_m2bc])
        rd_busy[s] = t2
        t_xn2 = P.emit("scalar", lambda e, o=xn2.ap()[:, c * T:(c + 1) * T],
                       i=scr6.ap(), sa=W["ln2_s"].ap()[:, c:c + 1],
                       ba=W["ln2_b"].ap()[:, c:c + 1]:
                       e.activation(o, i, AF.Identity, bias=ba, scale=sa), waits=[t2])
        eng_tok = t_xn2
    sc6.close()
    sc5.close()
    P.new_phase()

    sc7 = P.scope()
    hT = P.sbuf("hT", [128, 32 * 512], BF16, scope=sc7)
    w1_ring = Ring(P, "w1r", 2 * DC * 128, depth=3, scope=sc7)
    w2_ring = Ring(P, "w2r", 32 * 128, depth=3, scope=sc7)
    ph = P.psum("ph", [128, 2048], F32, scope=sc7)
    py = P.psum("py", [128, 2048], F32, scope=sc7)
    g_scr = [P.sbuf("gscr", [128, 512], F32, scope=sc7) for _ in range(6)]

    for tt in range(NT):
        tok0 = tt * 512
        evac_tok = [None] * 4
        for slab in range(16):
            si, slot, ready = w1_ring.load(
                W["W1"].ap()[slab * 2:(slab + 1) * 2].rearrange("f p n -> p f n"),
                out_view=lambda a: a.rearrange("p (f n) -> p f n", f=2))
            last_mm = None
            for fi in range(2):
                fc = slab * 2 + fi
                pb = fc % 4
                mm = None
                for k in range(DC):
                    mm = P.emit("tensor", lambda e,
                                o=ph.ap()[:, pb * 512:(pb + 1) * 512],
                                l=slot.ap()[:, fi * 1024 + k * 128: fi * 1024 + (k + 1) * 128],
                                r=xn2.ap()[:, k * T + tok0: k * T + tok0 + 512],
                                s_=(k == 0):
                                e.matmul(o, l, r, start=s_, stop=False, skip_group_check=True),
                                waits=[ready, t_xn2, evac_tok[pb]],
                                inc=True if k == DC - 1 else None)
                last_mm = mm
                gb = pb % 2 * 3
                sU, sA, sB = g_scr[gb], g_scr[gb + 1], g_scr[gb + 2]
                t0 = P.emit("vector", lambda e, o=sU.ap(),
                            i=ph.ap()[:, pb * 512:(pb + 1) * 512],
                            b_=W["b1"].ap()[:, fc:fc + 1]:
                            e.tensor_scalar(o, i, b_, None, ALU.add), waits=[mm])
                t1 = P.emit("scalar", lambda e, o=sA.ap(), i=sU.ap():
                            e.activation(o, i, AF.Square), waits=[t0])
                t2 = P.emit("vector", lambda e, o=sA.ap(), i=sA.ap():
                            e.tensor_scalar(o, i, GELU_C2, GELU_C, ALU.mult, ALU.add),
                            waits=[t1])
                t3 = P.emit("vector", lambda e, o=sB.ap(), a=sU.ap(), b=sA.ap():
                            e.tensor_mul(o, a, b), waits=[t2])
                t4 = P.emit("scalar", lambda e, o=sB.ap(), i=sB.ap():
                            e.activation(o, i, AF.Tanh), waits=[t3])
                t5 = P.emit("vector", lambda e, o=hT.ap()[:, fc * 512:(fc + 1) * 512],
                            i=sB.ap(), p_=sU.ap():
                            e.scalar_tensor_tensor(o, i, 1.0, p_, ALU.add, ALU.mult),
                            waits=[t4])
                evac_tok[pb] = t5
            w1_ring.free(si, last_mm)
        t_h = P.tok("vector")
        ev2 = [None] * 4
        for mc in range(DC):
            si, slot, ready = w2_ring.load(W["W2"].ap()[mc])
            pb = mc % 4
            mm = None
            for k in range(32):
                mm = P.emit("tensor", lambda e,
                            o=py.ap()[:, pb * 512:(pb + 1) * 512],
                            l=slot.ap()[:, k * 128:(k + 1) * 128],
                            r=hT.ap()[:, k * 512:(k + 1) * 512], s_=(k == 0):
                            e.matmul(o, l, r, start=s_, stop=False, skip_group_check=True),
                            waits=[ready, t_h, ev2[pb]],
                            inc=True if k == 31 else None)
            w2_ring.free(si, mm)
            sA = g_scr[pb % 2 * 3]
            t1 = P.emit("vector", lambda e, o=sA.ap(),
                        i=py.ap()[:, pb * 512:(pb + 1) * 512],
                        b_=W["b2"].ap()[:, mc:mc + 1]:
                        e.tensor_scalar(o, i, 0.5, b_, ALU.mult, ALU.add), waits=[mm])
            t2 = P.emit("vector", lambda e,
                        o=xT.ap()[:, mc * T + tok0: mc * T + tok0 + 512],
                        a=xT.ap()[:, mc * T + tok0: mc * T + tok0 + 512],
                        b=sA.ap(): e.tensor_add(o, a, b), waits=[t1, t_add_all])
            ev2[pb] = t2
    sc7.close()
    scxn.close()
    P.new_phase()
    return P.tok("vector")



def decl_l0attn_weights(nc, pfx="l0a"):
    dt = lambda n, s: nc.dram_tensor(pfx + "_" + n, s, F32, kind="ExternalInput")
    W = {}
    for n in ("ln1_s", "ln1_b", "bq", "bo", "b2", "ln2_s", "ln2_b"):
        W[n] = dt(n, [128, DC])
    W["b1"] = dt("b1", [128, 32])
    W["Wq"] = dt("wq", [DC, 128, DC * 128])
    W["Wo"] = dt("wo", [DC, 128, DC * 128])
    W["W1"] = dt("w1", [32, 128, DC * 128])
    W["W2"] = dt("w2", [DC, 128, 32 * 128])
    W["Wk"] = dt("wk", [2, 128, DC * 512])
    W["Wv"] = dt("wv", [2, 128, DC * 512])
    W["bk_bc"] = dt("bkbc", [128, 1024])
    W["bv_bc"] = dt("bvbc", [128, 1024])
    return W


def prep_l0attn_weights(i, pfx="l0a"):
    m = {}
    m[pfx + "_ln1_s"] = pdim(i["L0_attn_ln1_scale"])
    m[pfx + "_ln1_b"] = pdim(i["L0_attn_ln1_bias"])
    m[pfx + "_bq"] = pdim(np.asarray(i["L0_attn_bq"]).reshape(-1))
    m[pfx + "_bo"] = pdim(i["L0_attn_bo"])
    m[pfx + "_b2"] = pdim(i["L0_attn_b2"])
    m[pfx + "_b1"] = pdim(i["L0_attn_b1"])
    m[pfx + "_ln2_s"] = pdim(i["L0_attn_ln2_scale"])
    m[pfx + "_ln2_b"] = pdim(i["L0_attn_ln2_bias"])
    m[pfx + "_wq"] = chunk_w(np.asarray(i["L0_attn_Wq"]).reshape(D, D), D, D)
    m[pfx + "_wo"] = chunk_w(np.asarray(i["L0_attn_Wo"]).reshape(D, D), D, D)
    m[pfx + "_w1"] = chunk_w(i["L0_attn_W1"], D, 4096)
    m[pfx + "_w2"] = chunk_w(i["L0_attn_W2"], 4096, D)
    m[pfx + "_wk"] = kslab_half(np.asarray(i["L0_attn_Wk"]).reshape(D, D), D, D)
    m[pfx + "_wv"] = kslab_half(np.asarray(i["L0_attn_Wv"]).reshape(D, D), D, D)
    m[pfx + "_bkbc"] = row_repl(np.asarray(i["L0_attn_bk"]).reshape(-1))
    m[pfx + "_bvbc"] = row_repl(np.asarray(i["L0_attn_bv"]).reshape(-1))
    return m


CONST_SPECS = [
    ("ones_col_f32", (128, 1), F32),
    ("ones_col_bf", (128, 1), BF16),
    ("ones_1x128", (1, 128), F32),
    ("sel65", (65, 128), F32),
    ("ident", (128, 128), BF16),
    ("tri", (128, 128), BF16),
    ("e8sel", (8, 8 * 128), BF16),
    ("e16sel", (16, 16 * 128), BF16),
]


def load_consts(P, nc, specs):
    out = {}
    ctr = P.new_ctr("consts")
    n = 0
    for (name, shape, sdt) in specs:
        dr = nc.dram_tensor("cin_" + name, list(shape), F32, kind="ExternalInput").ap()
        h = P.sbuf("c_" + name, list(shape), sdt)
        q = "gpsimd" if sdt != F32 else "sync"
        P.emit(q, lambda e, o=h.ap(), i=dr: e.dma_start(out=o, in_=i),
               inc=(ctr, 16), nofence=True)
        n += 1
        out[name] = h
    out["_tok"] = (ctr, 16 * n)
    return out


def sbufize_params(P, W, items, ctr, scope=None):
    for key, shape, dt_ in items:
        dr = W[key]
        h = P.sbuf("p_" + key, list(shape), dt_, scope=scope)
        q = "gpsimd" if dt_ != F32 else "sync"
        P.emit(q, lambda e, o=h.ap(), i=dr.ap(): e.dma_start(out=o, in_=i),
               inc=(ctr, 16), nofence=True)
        W[key] = h


L0A_SMALL = [("ln1_s", (128, DC), F32), ("ln1_b", (128, DC), F32),
             ("bq", (128, DC), F32), ("bo", (128, DC), F32),
             ("b2", (128, DC), F32), ("ln2_s", (128, DC), F32),
             ("ln2_b", (128, DC), F32), ("b1", (128, 32), F32),
             ("bk_bc", (128, 1024), BF16),
             ("bv_bc", (128, 1024), BF16)]


def const_in_map():
    cst = make_consts()
    return {"cin_" + k: v for k, v in cst.items()}



def build_l0attn_program():
    nc = bass.Bass("TRN2", target_bir_lowering=False)
    ctx = contextlib.ExitStack()
    nc._kb_ctx = ctx
    xin = nc.dram_tensor("xin", [D, T], F32, kind="ExternalInput").ap()
    xout = nc.dram_tensor("xout", [D, T], F32, kind="ExternalOutput").ap()
    xres_dram = nc.dram_tensor("xres_scr", [D, T], F32, kind="Internal").ap()
    Wd = decl_l0attn_weights(nc)

    P = Prog(nc, ctx)
    consts = load_consts(P, nc, CONST_SPECS)
    pctr = P.new_ctr("params")
    sbufize_params(P, Wd, L0A_SMALL, pctr)

    xT = P.sbuf("xT", [128, DC * T], F32)
    ld = P.new_ctr("xld")
    P.emit("sync", lambda e: e.dma_start(
        out=xT.ap().rearrange("p (c t) -> p c t", c=DC),
        in_=xin.rearrange("(c p) t -> p c t", p=128)),
        inc=(ld, 16), nofence=True)
    P.fence_waits = [(ld, 16), consts["_tok"], (pctr, pctr.val)]

    bld_l0attn(P, xT, consts, Wd, xres_dram, x_tok=None)

    st = P.new_ctr("xst")
    P.emit("sync", lambda e: e.dma_start(
        out=xout.rearrange("(c p) t -> p c t", p=128),
        in_=xT.ap().rearrange("p (c t) -> p c t", c=DC)),
        inc=(st, 16))
    fin = [(st, 16)]
    if hasattr(P, "dbg_ctr"):
        fin.append((P.dbg_ctr, P.dbg_ctr.val))
    P.emit("sync", lambda e: e.nop(), waits=fin)

    block = ctx.enter_context(nc.Block())
    P.emit_all(block)
    ctx.close()
    return nc


def l0attn_in_map(inputs, x_core):
    m = prep_l0attn_weights(inputs)
    m["xin"] = np.ascontiguousarray(np.asarray(x_core, dtype=np.float32).T)
    m.update(const_in_map())
    return m


def kernel(**inputs):
    x = np.asarray(inputs["L0_attn_inputs"], dtype=np.float32)
    nc = build_l0attn_program()
    in_maps = [l0attn_in_map(inputs, x[c]) for c in range(NCORES)]
    res = run_bass_kernel_spmd(nc, in_maps, core_ids=list(range(NCORES)))
    out = np.stack([r["xout"].T for r in res.results], axis=0)
    return np.ascontiguousarray(out.astype(np.float32))



def outer_ln_rows(P, sc, xT, consts, x_tok, inner_eps=None):
    rows = ln_stats(P, sc, xT, consts, 1e-5, src_tok=x_tok)
    r_m0r0 = P.sbuf("r_m0r0", [1, T], F32, scope=sc)
    t_m0r0 = P.emit("vector", lambda e: e.tensor_mul(
        r_m0r0.ap(), rows["mu"].ap(), rows["rstd"].ap()),
        waits=[rows["t_mu"], rows["t_rstd"]])
    if inner_eps is None:
        return rows, None, None, (r_m0r0, t_m0r0)
    u1 = P.sbuf("r_u1", [1, T], F32, scope=sc)
    u2 = P.sbuf("r_u2", [1, T], F32, scope=sc)
    tv = rows["t_var"]
    t = P.emit("vector", lambda e: e.tensor_scalar(u1.ap(), rows["var"].ap(), 1e-5, None, ALU.add), waits=[tv])
    t = P.emit("vector", lambda e: e.reciprocal(u2.ap(), u1.ap()), waits=[t])
    t = P.emit("vector", lambda e: e.tensor_mul(u1.ap(), rows["var"].ap(), u2.ap()), waits=[t])
    t = P.emit("vector", lambda e: e.tensor_scalar(u2.ap(), u1.ap(), float(inner_eps), None, ALU.add), waits=[t])
    t = P.emit("scalar", lambda e: e.activation(u1.ap(), u2.ap(), AF.Ln), waits=[t])
    t_s = P.emit("scalar", lambda e: e.activation(u2.ap(), u1.ap(), AF.Exp, scale=-0.5), waits=[t])
    return rows, u2, t_s, (r_m0r0, t_m0r0)



def bld_l0ffn(P, xT, consts, W, x_tok=None):
    import os as _os
    dbg = bool(_os.environ.get("KB_DEBUG"))
    scpl = P.scope()
    rw_f = P.sbuf("rw_f", [128, 64], F32, scope=scpl)
    p_l = P.psum("p_l", [128, 1024], F32, scope=scpl)
    rwc = P.new_ctr("rwld")
    P.emit("sync", lambda e: e.dma_start(
        out=rw_f.ap().rearrange("p (k m) -> p k m", k=DC),
        in_=W["router"].ap().rearrange("k p m -> p k m")),
        inc=(rwc, 16))
    t_rw = (rwc, 16)
    sc0 = P.scope()
    sc_r = P.scope()
    rows, s_row, t_s, (r_m0r0, t_m0r0) = outer_ln_rows(P, sc_r, xT, consts,
                                                       x_tok, inner_eps=1e-5)
    bc = bcast_rows(P, rows["pst"], consts,
                    [(rows["rstd"], rows["t_rstd"], sc0), (r_m0r0, t_m0r0, sc0),
                     (s_row, t_s, sc0)],
                    prev_evac=rows["t_reads"])
    (r0_bc, t_r0bc), (m0_bc, t_m0bc), (s_bc, t_sbc) = bc
    t_xtin = None
    if dbg:
        P.dump("xTin", xT.ap(), [128, DC * T], F32, waits=[x_tok])
        t_xtin = (P.dbg_ctr, P.dbg_ctr.val)
        P.dump("r0bc", r0_bc.ap(), [128, T], F32, waits=[t_r0bc])
        P.dump("m0bc", m0_bc.ap(), [128, T], F32, waits=[t_m0bc])
        P.dump("sbc", s_bc.ap(), [128, T], F32, waits=[t_sbc])
    sc_r.close()

    scxn = P.scope()
    xn = P.sbuf("xn", [128, DC * T], BF16, scope=scxn)
    scrA = P.sbuf("scrA", [128, T], F32, scope=sc0)
    scrB = P.sbuf("scrB", [128, T], F32, scope=sc0)
    scrC = P.sbuf("scrC", [128, T], F32, scope=sc0)
    act_tok = None
    rmm = None
    for c in range(DC):
        t1 = P.emit("vector", lambda e, o=scrA.ap(), i=xT.ap()[:, c * T:(c + 1) * T],
                    r=r0_bc.ap(): e.tensor_mul(o, i, r),
                    waits=[t_r0bc, x_tok])
        t2 = P.emit("vector", lambda e, o=scrB.ap(), a=scrA.ap(), b=m0_bc.ap():
                    e.tensor_sub(o, a, b), waits=[t_m0bc, t1])
        t3 = P.emit("vector", lambda e, o=scrC.ap(), a=scrB.ap(), b=s_bc.ap():
                    e.tensor_mul(o, a, b), waits=[t_sbc, t2, act_tok, rmm])
        t3a = P.emit("vector", lambda e, o=scrA.ap(), i=scrC.ap(),
                     g=W["gamma"].ap()[:, c:c + 1]:
                     e.tensor_scalar(o, i, g, None, ALU.mult), waits=[t3])
        t3b = P.emit("vector", lambda e, o=scrC.ap(), i=scrA.ap(),
                     b_=W["beta"].ap()[:, c:c + 1]:
                     e.tensor_scalar(o, i, b_, None, ALU.add), waits=[t3a])
        act_tok = P.emit("scalar", lambda e, o=xn.ap()[:, c * T:(c + 1) * T],
                         i=scrC.ap(): e.copy(o, i), waits=[t3b])
        for tc in range(TC):
            c0 = 512 * (tc % 2) + 8 * (tc // 2)
            rmm = P.emit("tensor", lambda e, o=p_l.ap()[:, c0:c0 + 8],
                         l=scrC.ap()[:, tc * 128:(tc + 1) * 128],
                         r=rw_f.ap()[:, 8 * c:8 * c + 8],
                         s_=(c == 0 and tc < 2):
                         e.matmul(o, l, r, start=s_, stop=False, skip_group_check=True),
                         waits=[t3b, t_rw],
                         inc=True if tc == TC - 1 else None)
        P.emit("vector", lambda e, o=xT.ap()[:, c * T:(c + 1) * T],
               a=xT.ap()[:, c * T:(c + 1) * T], b=scrB.ap():
               e.tensor_add(o, a, b), waits=[t2, t_xtin])
    t_xn = act_tok
    if dbg:
        P.dump("xnf", xn.ap(), [128, DC * T], BF16, waits=[t_xn])
    t_lg = rmm
    t_adda = P.tok("vector")
    sc0.close()
    P.new_phase()

    scwd = P.scope()
    wdT_sb = P.sbuf("wdT", [8, T], BF16, scope=scwd, from_end=dbg)
    sc1 = P.scope()
    p_hist = P.sbuf("p_hist", [128, 128], F32, from_end=True) if dbg else None
    l_hist = P.sbuf("l_hist", [128, 128], F32, from_end=True) if dbg else None
    wd_bf = P.sbuf("wd_bf", [128, 128], BF16, scope=None if dbg else sc1,
                   from_end=dbg)
    t8 = P.sbuf("t8", [128, 8], F32, scope=sc1)
    e8_ = P.sbuf("e8_", [128, 8], F32, scope=sc1)
    p8 = P.sbuf("p8", [128, 8], F32, scope=sc1)
    q8 = P.sbuf("q8", [128, 8], F32, scope=sc1)
    mcol = P.sbuf("mcol", [128, 1], F32, scope=sc1)
    scol = P.sbuf("scol", [128, 1], F32, scope=sc1)
    rcol = P.sbuf("rcol", [128, 1], F32, scope=sc1)
    wdT_ps = P.psum("wdT_ps", [128, 2048], BF16, scope=sc1)

    tr_tok = None
    for tc in range(TC):
        c0 = 512 * (tc % 2) + 8 * (tc // 2)
        sl = p_l.ap()[:, c0:c0 + 8]
        sl = p_l.ap()[:, c0:c0 + 8]
        t1 = P.emit("vector", lambda e, o=mcol.ap(), i=sl: e.reduce_max(
            o, i, axis=mybir.AxisListType.X), waits=[t_lg])
        t2 = P.emit("vector", lambda e, o=t8.ap(), i=sl, m=mcol.ap():
                    e.tensor_scalar(o, i, m, 1.0 / 1.5, ALU.subtract, ALU.mult),
                    waits=[t1])
        t3 = P.emit("scalar", lambda e: e.activation(e8_.ap(), t8.ap(), AF.Exp),
                    waits=[t2])
        t4 = P.emit("vector", lambda e: e.reduce_sum(scol.ap(), e8_.ap(),
                                                     axis=mybir.AxisListType.X),
                    waits=[t3])
        t5 = P.emit("vector", lambda e: e.reciprocal(rcol.ap(), scol.ap()), waits=[t4])
        t6 = P.emit("vector", lambda e: e.tensor_scalar(p8.ap(), e8_.ap(), rcol.ap(),
                                                        None, ALU.mult), waits=[t5])
        if dbg:
            P.emit("vector", lambda e, o=p_hist.ap()[:, 8 * tc:8 * tc + 8]:
                   e.tensor_copy(o, p8.ap()), waits=[t6])
            P.emit("vector", lambda e, o=l_hist.ap()[:, 8 * tc:8 * tc + 8], i=sl:
                   e.tensor_copy(o, i), waits=[t_lg])
        t7 = P.emit("vector", lambda e: e.reduce_max(mcol.ap(), p8.ap(),
                                                     axis=mybir.AxisListType.X),
                    waits=[t6])
        t8_ = P.emit("vector", lambda e: e.tensor_scalar(q8.ap(), p8.ap(), mcol.ap(),
                                                         None, ALU.is_equal), waits=[t7])
        t9 = P.emit("vector", lambda e: e.scalar_tensor_tensor(
            q8.ap(), q8.ap(), -1e30, p8.ap(), ALU.mult, ALU.add), waits=[t8_])
        t10 = P.emit("vector", lambda e: e.reduce_max(mcol.ap(), q8.ap(),
                                                      axis=mybir.AxisListType.X),
                     waits=[t9])
        t11 = P.emit("vector", lambda e: e.tensor_scalar(q8.ap(), p8.ap(), mcol.ap(),
                                                         None, ALU.is_ge), waits=[t10])
        t12 = P.emit("vector", lambda e, o=wd_bf.ap()[:, 8 * tc:8 * tc + 8]:
                     e.tensor_mul(o, p8.ap(), q8.ap()), waits=[t11])
        tr_tok = P.emit("tensor", lambda e,
                        o=wdT_ps.ap()[0:8, 128 * tc:128 * (tc + 1)],
                        i=wd_bf.ap()[:, 8 * tc:8 * tc + 8]:
                        e.transpose(o, i, consts["ident"].ap()),
                        waits=[t12], inc=True)
    t_wdT = P.emit("scalar", lambda e: e.copy(wdT_sb.ap(), wdT_ps.ap()[0:8, :]),
                   waits=[tr_tok])
    sc1.close()
    scpl.close()
    P.new_phase()

    sc2 = P.scope()
    hT = P.sbuf("hTf", [128, 32 * 1024], BF16, scope=None if dbg else sc2)
    we_sb = P.sbuf("we_sb", [128, T], BF16, scope=None if dbg else sc2,
                   from_end=dbg)
    scrE = [P.sbuf("scrE", [128, 1024], F32, scope=sc2) for _ in range(2)]
    w1_ring = Ring(P, "w1fr", 2 * 1024, depth=2, scope=sc2)
    w2_ring = Ring(P, "w2fr", 2048, depth=2, scope=sc2)
    ph = P.psum("phf", [128, 2048], F32, scope=sc2)
    py = P.psum("pyf", [128, 2048], F32, scope=sc2)

    last_evac = None
    NEXP = int(_os.environ.get("KB_L0F_EXPERTS", "8"))
    for e_ in range(NEXP):
        bmm = None
        for nt in range(NT):
            bmm = P.emit("tensor", lambda e, o=py.ap()[:, nt * 512:(nt + 1) * 512],
                         l=consts["e8sel"].ap()[:, 128 * e_:128 * (e_ + 1)],
                         r=wdT_sb.ap()[:, nt * 512:(nt + 1) * 512]:
                         e.matmul(o, l, r, start=True, stop=True, skip_group_check=True),
                         waits=[t_wdT, last_evac], inc=nt == NT - 1)
        t_we = P.emit("scalar", lambda e: e.copy(we_sb.ap(), py.ap()),
                      waits=[bmm])
        ev2 = [t_we, t_we]
        for tt in range(2):
            tok0 = tt * 1024
            evac_tok = [None, None]
            for slab in range(16):
                si, slot, ready = w1_ring.load(
                    W["W1"].ap()[e_, slab * 2:(slab + 1) * 2].rearrange("f p n -> p f n"),
                    out_view=lambda a: a.rearrange("p (f n) -> p f n", f=2))
                last_mm = None
                for fi in range(2):
                    fc = slab * 2 + fi
                    pb = fc % 2
                    mm = None
                    for k in range(DC):
                        for hh in range(2):
                            mm = P.emit("tensor", lambda e,
                                        o=ph.ap()[:, pb * 1024 + hh * 512: pb * 1024 + (hh + 1) * 512],
                                        l=slot.ap()[:, fi * 1024 + k * 128: fi * 1024 + (k + 1) * 128],
                                        r=xn.ap()[:, k * T + tok0 + hh * 512: k * T + tok0 + (hh + 1) * 512],
                                        s_=(k == 0):
                                        e.matmul(o, l, r, start=s_, stop=False, skip_group_check=True),
                                        waits=[ready, t_xn, evac_tok[pb]],
                                        inc=True if (k == DC - 1 and hh == 1) else None)
                    last_mm = mm
                    evac_tok[pb] = P.emit("scalar", lambda e,
                                          o=hT.ap()[:, fc * 1024:(fc + 1) * 1024],
                                          i=ph.ap()[:, pb * 1024:(pb + 1) * 1024]:
                                          e.activation(o, i, AF.Silu), waits=[mm])
                w1_ring.free(si, last_mm)
            t_h = P.tok("scalar")
            for mc in range(DC):
                pb = mc % 2
                mm = None
                for half in range(2):
                    si, slot, ready = w2_ring.load(
                        W["W2"].ap()[e_, mc][:, half * 2048:(half + 1) * 2048])
                    for kk in range(16):
                        k = half * 16 + kk
                        for hh in range(2):
                            last = (kk == 15 and hh == 1)
                            mm = P.emit("tensor", lambda e,
                                        o=py.ap()[:, pb * 1024 + hh * 512: pb * 1024 + (hh + 1) * 512],
                                        l=slot.ap()[:, kk * 128:(kk + 1) * 128],
                                        r=hT.ap()[:, k * 1024 + hh * 512: k * 1024 + (hh + 1) * 512],
                                        s_=(k == 0):
                                        e.matmul(o, l, r, start=s_, stop=False, skip_group_check=True),
                                        waits=[ready, t_h, ev2[pb]],
                                        inc=True if last else None)
                    w2_ring.free(si, mm)
                t1 = P.emit("vector", lambda e, o=scrE[pb].ap(),
                            i=py.ap()[:, pb * 1024:(pb + 1) * 1024],
                            w_=we_sb.ap()[:, tok0:tok0 + 1024]:
                            e.tensor_mul(o, i, w_), waits=[mm, t_we])
                t2 = P.emit("vector", lambda e,
                            o=xT.ap()[:, mc * T + tok0: mc * T + tok0 + 1024],
                            a=xT.ap()[:, mc * T + tok0: mc * T + tok0 + 1024],
                            b=scrE[pb].ap(): e.tensor_add(o, a, b),
                            waits=[t1, t_adda])
                ev2[pb] = t2
        last_evac = ev2[0] if ev2[0] else ev2[1]
        last_evac = ("multi", [ev2[0], ev2[1]])
    sc2.close()
    scwd.close()
    scxn.close()
    P.new_phase()
    if dbg:
        P.dump("p_hist", p_hist.ap(), [128, 128], F32)
        P.dump("l_hist", l_hist.ap(), [128, 128], F32)
        P.dump("wd_bf", wd_bf.ap(), [128, 128], BF16)
        P.dump("we7", we_sb.ap(), [128, T], BF16)
        P.dump("hT7", hT.ap(), [128, 32 * 1024], BF16)
    return P.tok("vector")


def decl_l0ffn_weights(nc, pfx="l0f"):
    dt = lambda n, s: nc.dram_tensor(pfx + "_" + n, s, F32, kind="ExternalInput")
    W = {}
    W["gamma"] = dt("gamma", [128, DC])
    W["beta"] = dt("beta", [128, DC])
    W["router"] = dt("router", [DC, 128, 8])
    W["W1"] = dt("w1", [8, 32, 128, DC * 128])
    W["W2"] = dt("w2", [8, DC, 128, 32 * 128])
    return W


def prep_l0ffn_weights(i, pfx="l0f"):
    m = {}
    m[pfx + "_gamma"] = pdim(i["L0_ffn_gamma"])
    m[pfx + "_beta"] = pdim(i["L0_ffn_beta"])
    rwT = np.ascontiguousarray(np.asarray(i["L0_ffn_router_w"], np.float32).T)
    m[pfx + "_router"] = np.ascontiguousarray(rwT.reshape(DC, 128, 8))
    m[pfx + "_w1"] = np.stack([chunk_w(np.asarray(i["L0_ffn_W1"])[e], D, 4096)
                               for e in range(8)])
    m[pfx + "_w2"] = np.stack([chunk_w(np.asarray(i["L0_ffn_W2"])[e], 4096, D)
                               for e in range(8)])
    return m


L0F_SMALL = [("gamma", (128, DC), F32), ("beta", (128, DC), F32)]



def bld_l1attn(P, xT, consts, W, x_tok=None):
    sc0 = P.scope()
    sc_r = P.scope()
    rows, _, _, (r_m0r0, t_m0r0) = outer_ln_rows(P, sc_r, xT, consts, x_tok)
    bc = bcast_rows(P, rows["pst"], consts,
                    [(rows["rstd"], rows["t_rstd"], sc0), (r_m0r0, t_m0r0, sc0)],
                    prev_evac=rows["t_reads"])
    (r0_bc, t_r0bc), (m0_bc, t_m0bc) = bc
    sc_r.close()
    sca = P.scope()
    aT = P.sbuf("aT", [128, DC * T], BF16, scope=sca)
    scr0 = P.sbuf("scr0", [128, T], F32, scope=sc0)
    t_a = ln_apply(P, xT, r0_bc, t_r0bc, m0_bc, t_m0bc, aT, None, None, scr0,
                   src_tok=x_tok)
    sc0.close()
    P.new_phase()

    scq = P.scope()
    qT = P.sbuf("qT", [128, DC * T], BF16, scope=scq)
    sc1 = P.scope()
    wq_ring = Ring(P, "wq1", DC * 128, depth=2, scope=sc1)
    pq = [P.psum("pq1", [128, T], F32, scope=sc1) for _ in range(2)]

    def evac_q(mc, mm_tok):
        return P.emit("scalar", lambda e, o=qT.ap()[:, mc * T:(mc + 1) * T],
                      i=pq[mc % 2].ap(): e.copy(o, i), waits=[mm_tok])

    ws_matmul(P, wq_ring, lambda mc: W["Wq"].ap()[mc],
              lambda k, nt: aT.ap()[:, k * T + nt * 512: k * T + (nt + 1) * 512],
              [[pq[0].ap()[:, nt * 512:(nt + 1) * 512] for nt in range(NT)],
               [pq[1].ap()[:, nt * 512:(nt + 1) * 512] for nt in range(NT)]],
              evac_q, range(DC), DC, NT, rhs_toks=[t_a])
    t_qT = P.tok("scalar")
    sc1.close()
    P.new_phase()

    sco = P.scope()
    oT = P.sbuf("oT", [128, DC * T], BF16, scope=sco)
    sc2 = P.scope()
    khT = P.sbuf("khT", [128, T], BF16, scope=sc2)
    vh = P.sbuf("vh", [128, 16, 65], BF16, scope=sc2)
    probs = [P.sbuf("probs", [128, 1024], BF16, scope=sc2) for _ in range(2)]
    recipb = P.sbuf("recipb1", [65, 1024], F32, scope=sc2)
    rbc = P.sbuf("rbc1", [128, 1024], F32, scope=sc2)
    wkh_ring = Ring(P, "wkh", 512, depth=3, scope=sc2)
    wvh_ring = Ring(P, "wvh", 512, depth=3, scope=sc2)
    t_zz = P.emit("vector", lambda e: e.memset(recipb.ap(), 0.0))
    t_ones1 = P.emit("vector", lambda e: e.memset(vh.ap()[:, :, 64:65], 1.0))

    import os as _os
    NHEADS = int(_os.environ.get("KB_L1A_HEADS", "16"))
    prev_khT_use = None
    prev_vh_use = None
    prev_probs_use = [None, None]
    t_omult = None
    for h in range(NHEADS):
        p64 = 64 * (h % 2)
        sc_h = P.scope()
        pk = P.psum("pk1", [128, T], F32, scope=sc_h)
        si_k, wk_slot, wk_rdy = wkh_ring.load(W["Wk"].ap()[h])
        mm = None
        for k in range(DC):
            for nt in range(NT):
                mm = P.emit("tensor", lambda e,
                            o=pk.ap()[p64:p64 + 64, nt * 512:(nt + 1) * 512],
                            l=wk_slot.ap()[:, k * 64:(k + 1) * 64],
                            r=aT.ap()[:, k * T + nt * 512: k * T + (nt + 1) * 512],
                            s_=(k == 0):
                            e.matmul(o, l, r, start=s_, stop=False, skip_group_check=True),
                            waits=[wk_rdy, t_a, t_omult],
                            inc=True if (k == DC - 1 and nt == NT - 1) else None)
        wkh_ring.free(si_k, mm)
        t_kh = P.emit("scalar", lambda e, o=khT.ap()[p64:p64 + 64, :],
                      i=pk.ap()[p64:p64 + 64, :]: e.copy(o, i),
                      waits=[mm, prev_khT_use])
        sc_h.close()
        sc_v = P.scope()
        pv = P.psum("pv1", [128, 1024], F32, scope=sc_v)
        si_v, wv_slot, wv_rdy = wvh_ring.load(W["Wv"].ap()[h])
        ev_v = prev_vh_use
        grp_ev = [None, None]
        for tc in range(TC):
            c0 = 512 * (tc % 2) + 64 * (tc // 2)
            mm = None
            for k in range(DC):
                mm = P.emit("tensor", lambda e, o=pv.ap()[:, c0:c0 + 64],
                            l=aT.ap()[:, k * T + tc * 128: k * T + tc * 128 + 128],
                            r=wv_slot.ap()[:, k * 64:(k + 1) * 64],
                            s_=(k == 0 and tc < 2):
                            e.matmul(o, l, r, start=s_, stop=False,
                                     skip_group_check=True),
                            waits=[wv_rdy, t_a, grp_ev[tc % 2], ev_v, t_kh],
                            inc=True if k == DC - 1 else None)
            grp_ev[tc % 2] = P.emit("vector", lambda e, o=vh.ap()[:, tc, 0:64],
                                    i=pv.ap()[:, c0:c0 + 64]: e.tensor_copy(o, i),
                                    waits=[mm, t_ones1])
        wvh_ring.free(si_v, ("multi", [grp_ev[0], grp_ev[1]]))
        t_vh = P.tok("vector")
        sc_v.close()
        sc_s = P.scope()
        p_s = [P.psum("ps1", [128, 1024], F32, scope=sc_s) for _ in range(2)]
        p_o = P.psum("po1", [128, 1024], F32, scope=sc_s)
        for half in range(2):
            tq0 = 1024 * half
            ntk = 8 * (half + 1)
            last_av = None
            g0 = t_vh if half == 0 else t_omult
            exp_tok = [g0, g0]

            def emit_scores(tkc):
                sl = tkc % 2
                lc0 = max(0, 128 * tkc - tq0)
                mm_s = None
                col = lc0
                while col < 1024:
                    w_ = min(512 - (col % 512), 1024 - col)
                    mm_s = P.emit("tensor", lambda e,
                                  o=p_s[sl].ap()[:, col:col + w_],
                                  l=khT.ap()[p64:p64 + 64, 128 * tkc:128 * tkc + 128],
                                  r=qT.ap()[p64:p64 + 64, (h // 2) * T + tq0 + col:
                                            (h // 2) * T + tq0 + col + w_]:
                                  e.matmul(o, l, r, start=True, stop=True,
                                           skip_group_check=True),
                                  waits=[t_kh, t_qT, exp_tok[sl]],
                                  inc=col + w_ >= 1024)
                    col += w_
                t_exp = P.emit("scalar", lambda e,
                               o=probs[sl].ap()[:, lc0:1024],
                               i=p_s[sl].ap()[:, lc0:1024]:
                               e.activation(o, i, AF.Exp, scale=0.125),
                               waits=[mm_s, prev_probs_use[sl]])
                if 128 * tkc >= tq0:
                    t_exp = P.emit("vector", lambda e,
                                   o=probs[sl].ap()[:, lc0:lc0 + 128],
                                   a=probs[sl].ap()[:, lc0:lc0 + 128],
                                   b=consts["tri"].ap(): e.tensor_mul(o, a, b),
                                   waits=[t_exp])
                exp_tok[sl] = t_exp
                prev_probs_use[sl] = None
                return t_exp, lc0, sl

            pend = emit_scores(0)
            for tkc in range(ntk):
                nxt = emit_scores(tkc + 1) if tkc + 1 < ntk else None
                t_exp, lc0, sl = pend
                col = lc0
                while col < 1024:
                    w_ = min(512 - (col % 512), 1024 - col)
                    last_av = P.emit("tensor", lambda e,
                                     o=p_o.ap()[0:65, col:col + w_],
                                     l=vh.ap()[:, tkc, :],
                                     r=probs[sl].ap()[:, col:col + w_],
                                     s_=(tkc == 0):
                                     e.matmul(o, l, r, start=s_, stop=False,
                                              skip_group_check=True),
                                     waits=[t_vh, t_exp, t_omult],
                                     inc=col + w_ >= 1024)
                    col += w_
                prev_probs_use[sl] = last_av
                pend = nxt
            t_r = P.emit("vector", lambda e: e.reciprocal(
                recipb.ap()[64:65, :], p_o.ap()[64:65, 0:1024]), waits=[last_av, t_zz])
            bcmm = None
            for nt in range(2):
                bcmm = P.emit("tensor", lambda e,
                              o=p_s[0].ap()[:, nt * 512:(nt + 1) * 512],
                              l=consts["sel65"].ap(),
                              r=recipb.ap()[0:65, nt * 512:(nt + 1) * 512]:
                              e.matmul(o, l, r, start=True, stop=True,
                                       skip_group_check=True),
                              waits=[t_r], inc=nt == 1)
            t_bc = P.emit("scalar", lambda e: e.copy(rbc.ap(), p_s[0].ap()[:, 0:1024]),
                          waits=[bcmm])
            if h % 2 == 0:
                dst = oT.ap()[0:64, (h // 2) * T + tq0:(h // 2) * T + tq0 + 1024]
            else:
                dst = qT.ap()[0:64, (h // 2) * T + tq0:(h // 2) * T + tq0 + 1024]
            t_omult = P.emit("vector", lambda e, o=dst:
                             e.tensor_mul(o, p_o.ap()[0:64, 0:1024],
                                          rbc.ap()[0:64, :]), waits=[t_bc])
            prev_probs_use = [t_omult, t_omult]
        sc_s.close()
        prev_khT_use = t_omult
        prev_vh_use = t_omult
    mvo = P.new_ctr("omv")
    P.emit("sync", lambda e: e.dma_start(out=oT.ap()[64:128, :], in_=qT.ap()[0:64, :]),
           waits=[t_omult], inc=(mvo, 16))
    t_oT = (mvo, 16)
    sc2.close()
    scq.close()
    sca.close()
    P.new_phase()

    sc3 = P.scope()
    wo_ring = Ring(P, "wo1", DC * 128, depth=4, scope=sc3)
    pwo = [P.psum("pwo1", [128, T], F32, scope=sc3) for _ in range(2)]

    def evac_o(mc, mm_tok):
        return P.emit("vector", lambda e, o=xT.ap()[:, mc * T:(mc + 1) * T],
                      a=xT.ap()[:, mc * T:(mc + 1) * T], i=pwo[mc % 2].ap():
                      e.tensor_tensor(o, a, i, ALU.add), waits=[mm_tok, x_tok])

    ws_matmul(P, wo_ring, lambda mc: W["Wout"].ap()[mc],
              lambda k, nt: oT.ap()[:, k * T + nt * 512: k * T + (nt + 1) * 512],
              [[pwo[0].ap()[:, nt * 512:(nt + 1) * 512] for nt in range(NT)],
               [pwo[1].ap()[:, nt * 512:(nt + 1) * 512] for nt in range(NT)]],
              evac_o, range(DC), DC, NT, rhs_toks=[t_oT])
    sc3.close()
    sco.close()
    P.new_phase()
    return P.tok("vector")


def decl_l1attn_weights(nc, pfx="l1a"):
    dt = lambda n, s: nc.dram_tensor(pfx + "_" + n, s, F32, kind="ExternalInput")
    W = {}
    W["Wq"] = dt("wq", [DC, 128, DC * 128])
    W["Wk"] = dt("wk", [16, 128, DC * 64])
    W["Wv"] = dt("wv", [16, 128, DC * 64])
    W["Wout"] = dt("wout", [DC, 128, DC * 128])
    return W


def head_chunk(w):
    w = np.asarray(w, np.float32).reshape(DC, 128, 16, 64)
    return np.ascontiguousarray(w.transpose(2, 1, 0, 3).reshape(16, 128, DC * 64))


def prep_l1attn_weights(i, pfx="l1a"):
    m = {}
    m[pfx + "_wq"] = chunk_w(np.asarray(i["L1_attn_Wq"]).reshape(D, D), D, D)
    m[pfx + "_wk"] = head_chunk(i["L1_attn_Wk"])
    m[pfx + "_wv"] = head_chunk(i["L1_attn_Wv"])
    m[pfx + "_wout"] = chunk_w(np.asarray(i["L1_attn_Wout"]).reshape(D, D), D, D)
    return m



def bld_l1ffn(P, xT, consts, W, x_tok=None):
    scpl = P.scope()
    rw_f = P.sbuf("rw_f1", [128, 128], F32, scope=scpl)
    p_l = P.psum("p_l1", [128, 1024], F32, scope=scpl)
    rwc = P.new_ctr("rwld1")
    P.emit("sync", lambda e: e.dma_start(
        out=rw_f.ap().rearrange("p (k m) -> p k m", k=DC),
        in_=W["router"].ap().rearrange("k p m -> p k m")),
        inc=(rwc, 16))
    t_rw = (rwc, 16)
    sc0 = P.scope()
    sc_r = P.scope()
    rows, _, _, (r_m0r0, t_m0r0) = outer_ln_rows(P, sc_r, xT, consts, x_tok)
    bc = bcast_rows(P, rows["pst"], consts,
                    [(rows["rstd"], rows["t_rstd"], sc0), (r_m0r0, t_m0r0, sc0)],
                    prev_evac=rows["t_reads"])
    (r0_bc, t_r0bc), (m0_bc, t_m0bc) = bc
    sc_r.close()
    sch = P.scope()
    hid = P.sbuf("hid", [128, DC * T], BF16, scope=sch)
    scr0 = P.sbuf("scr0f", [128, T], F32, scope=sc0)
    scr0b = P.sbuf("scr0g", [128, T], F32, scope=sc0)
    t_hid = None
    rmm = None
    for c in range(DC):
        t1 = P.emit("vector", lambda e, o=scr0.ap(), i=xT.ap()[:, c * T:(c + 1) * T],
                    r=r0_bc.ap(): e.tensor_mul(o, i, r),
                    waits=[t_r0bc, x_tok])
        t2 = P.emit("vector", lambda e, o=scr0b.ap(), a=scr0.ap(), b=m0_bc.ap():
                    e.tensor_sub(o, a, b), waits=[t_m0bc, t1, t_hid, rmm])
        t_hid = P.emit("scalar", lambda e, o=hid.ap()[:, c * T:(c + 1) * T],
                       i=scr0b.ap(): e.copy(o, i), waits=[t2])
        for tc in range(TC):
            c0 = 512 * (tc % 2) + 16 * (tc // 2)
            rmm = P.emit("tensor", lambda e, o=p_l.ap()[:, c0:c0 + 16],
                         l=scr0b.ap()[:, tc * 128:(tc + 1) * 128],
                         r=rw_f.ap()[:, 16 * c:16 * c + 16],
                         s_=(c == 0 and tc < 2):
                         e.matmul(o, l, r, start=s_, stop=False, skip_group_check=True),
                         waits=[t2, t_rw],
                         inc=True if tc == TC - 1 else None)
    t_lg = rmm
    sc0.close()
    P.new_phase()

    scsd = P.scope()
    sdT_sb = P.sbuf("sdT", [16, T], BF16, scope=scsd)
    sc1 = P.scope()
    sd_bf = P.sbuf("sd_bf", [128, 256], BF16, scope=sc1)
    s16 = P.sbuf("s16", [128, 16], F32, scope=sc1)
    q16 = P.sbuf("q16", [128, 16], F32, scope=sc1)
    mcol = P.sbuf("mcol1", [128, 1], F32, scope=sc1)
    sdT_ps = P.psum("sdT_ps", [128, 2048], BF16, scope=sc1)

    tr_tok = None
    for tc in range(TC):
        c0 = 512 * (tc % 2) + 16 * (tc // 2)
        sl = p_l.ap()[:, c0:c0 + 16]
        t1 = P.emit("vector", lambda e, o=mcol.ap(), i=sl: e.reduce_max(
            o, i, axis=mybir.AxisListType.X), waits=[t_lg])
        t2 = P.emit("vector", lambda e, o=q16.ap(), i=sl, m=mcol.ap():
                    e.tensor_scalar(o, i, m, None, ALU.is_equal), waits=[t1])
        t3 = P.emit("scalar", lambda e, o=s16.ap(), i=sl:
                    e.activation(o, i, AF.Sigmoid), waits=[t1])
        t4 = P.emit("vector", lambda e, o=sd_bf.ap()[:, 16 * tc:16 * tc + 16]:
                    e.tensor_mul(o, s16.ap(), q16.ap()), waits=[t2, t3])
        tr_tok = P.emit("tensor", lambda e,
                        o=sdT_ps.ap()[0:16, 128 * tc:128 * (tc + 1)],
                        i=sd_bf.ap()[:, 16 * tc:16 * tc + 16]:
                        e.transpose(o, i, consts["ident"].ap()),
                        waits=[t4], inc=True)
    t_sdT = P.emit("scalar", lambda e: e.copy(sdT_sb.ap(), sdT_ps.ap()[0:16, :]),
                   waits=[tr_tok])
    sc1.close()
    scpl.close()
    P.new_phase()

    sc2 = P.scope()
    sT = P.sbuf("sT", [128, 32 * 1024], BF16, scope=sc2)
    g_scr = P.sbuf("g_scr1", [128, 1024], F32, scope=sc2)
    wg_ring = Ring(P, "wg1", 2 * 1024, depth=2, scope=sc2)
    wu_ring = Ring(P, "wu1", 2 * 1024, depth=2, scope=sc2)
    wd_ring = Ring(P, "wd1", 2048, depth=2, scope=sc2)
    psA = P.psum("psA", [128, 1024], F32, scope=sc2)
    psB = P.psum("psB", [128, 1024], F32, scope=sc2)
    pyy = P.psum("pyy", [128, 2048], F32, scope=sc2)

    for tt in range(2):
        tok0 = tt * 1024
        ev_g = None
        ev_u = None
        act_tok = None
        for slab in range(16):
            si_g, gslot, g_rdy = wg_ring.load(
                W["sgate"].ap()[slab * 2:(slab + 1) * 2].rearrange("f p n -> p f n"),
                out_view=lambda a: a.rearrange("p (f n) -> p f n", f=2))
            si_u, uslot, u_rdy = wu_ring.load(
                W["sup"].ap()[slab * 2:(slab + 1) * 2].rearrange("f p n -> p f n"),
                out_view=lambda a: a.rearrange("p (f n) -> p f n", f=2))
            mmg_last = mmu_last = None
            for fi in range(2):
                fc = slab * 2 + fi
                mmg = mmu = None
                for k in range(DC):
                    for hh in range(2):
                        mmg = P.emit("tensor", lambda e,
                                     o=psA.ap()[:, hh * 512:(hh + 1) * 512],
                                     l=gslot.ap()[:, fi * 1024 + k * 128: fi * 1024 + (k + 1) * 128],
                                     r=hid.ap()[:, k * T + tok0 + hh * 512: k * T + tok0 + (hh + 1) * 512],
                                     s_=(k == 0):
                                     e.matmul(o, l, r, start=s_, stop=False, skip_group_check=True),
                                     waits=[g_rdy, t_hid, ev_g],
                                     inc=True if (k == DC - 1 and hh == 1) else None)
                for k in range(DC):
                    for hh in range(2):
                        mmu = P.emit("tensor", lambda e,
                                     o=psB.ap()[:, hh * 512:(hh + 1) * 512],
                                     l=uslot.ap()[:, fi * 1024 + k * 128: fi * 1024 + (k + 1) * 128],
                                     r=hid.ap()[:, k * T + tok0 + hh * 512: k * T + tok0 + (hh + 1) * 512],
                                     s_=(k == 0):
                                     e.matmul(o, l, r, start=s_, stop=False, skip_group_check=True),
                                     waits=[u_rdy, t_hid, ev_u],
                                     inc=True if (k == DC - 1 and hh == 1) else None)
                t1 = P.emit("scalar", lambda e: e.activation(g_scr.ap(), psA.ap(),
                                                             AF.Silu),
                            waits=[mmg, act_tok])
                ev_g = t1
                act_tok = P.emit("vector", lambda e,
                                 o=sT.ap()[:, fc * 1024:(fc + 1) * 1024],
                                 a=g_scr.ap(), b=psB.ap():
                                 e.tensor_tensor(o, a, b, ALU.mult),
                                 waits=[t1, mmu])
                ev_u = act_tok
                mmg_last, mmu_last = mmg, mmu
            wg_ring.free(si_g, mmg_last)
            wu_ring.free(si_u, mmu_last)
        t_s = P.tok("vector")
        ev_y = [None, None]
        for mc in range(DC):
            pb = mc % 2
            mm = None
            for half in range(2):
                si_d, dslot, d_rdy = wd_ring.load(
                    W["sdown"].ap()[mc][:, half * 2048:(half + 1) * 2048])
                for kk in range(16):
                    k = half * 16 + kk
                    for hh in range(2):
                        last = (kk == 15 and hh == 1)
                        mm = P.emit("tensor", lambda e,
                                    o=pyy.ap()[:, pb * 1024 + hh * 512: pb * 1024 + (hh + 1) * 512],
                                    l=dslot.ap()[:, kk * 128:(kk + 1) * 128],
                                    r=sT.ap()[:, k * 1024 + hh * 512: k * 1024 + (hh + 1) * 512],
                                    s_=(k == 0):
                                    e.matmul(o, l, r, start=s_, stop=False, skip_group_check=True),
                                    waits=[d_rdy, t_s, ev_y[pb]],
                                    inc=True if last else None)
                wd_ring.free(si_d, mm)
            ev_y[pb] = P.emit("vector", lambda e,
                              o=xT.ap()[:, mc * T + tok0: mc * T + tok0 + 1024],
                              a=xT.ap()[:, mc * T + tok0: mc * T + tok0 + 1024],
                              i=pyy.ap()[:, pb * 1024:(pb + 1) * 1024]:
                              e.tensor_tensor(o, a, i, ALU.add),
                              waits=[mm, x_tok])
    sc2.close()
    P.new_phase()

    sc3 = P.scope()
    scaled = P.sbuf("scaled", [128, DC * T], BF16, scope=sc3)
    actb = P.sbuf("actb", [128, 8 * T], BF16, scope=sc3)
    se_sb = P.sbuf("se_sb", [128, T], BF16, scope=sc3)
    g_scr2 = P.sbuf("g_scr2", [128, T], F32, scope=sc3)
    gu_ring = Ring(P, "gur", 1024, depth=4, scope=sc3)
    dn_ring = Ring(P, "dnr", 2 * 1024, depth=3, scope=sc3)
    psA2 = P.psum("psA2", [128, 2048], F32, scope=sc3)
    psB2 = P.psum("psB2", [128, 2048], F32, scope=sc3)

    prev_done = None
    for n in range(16):
        bmm = None
        for nt in range(NT):
            bmm = P.emit("tensor", lambda e, o=psA2.ap()[:, nt * 512:(nt + 1) * 512],
                         l=consts["e16sel"].ap()[:, 128 * n:128 * (n + 1)],
                         r=sdT_sb.ap()[:, nt * 512:(nt + 1) * 512],
                         s_=True:
                         e.matmul(o, l, r, start=s_, stop=True, skip_group_check=True),
                         waits=[t_sdT, prev_done], inc=nt == NT - 1)
        t_se = P.emit("scalar", lambda e: e.copy(se_sb.ap(), psA2.ap()),
                      waits=[bmm])
        t_sc = None
        for c in range(DC):
            t_sc = P.emit("vector", lambda e, o=scaled.ap()[:, c * T:(c + 1) * T],
                          a=hid.ap()[:, c * T:(c + 1) * T], b=se_sb.ap():
                          e.tensor_tensor(o, a, b, ALU.mult),
                          waits=[t_se, t_hid, prev_done])
        ev_g = [t_se, t_se]
        ev_u = [None, None]
        for j in range(8):
            si_g, gslot, g_rdy = gu_ring.load(
                W["gup"].ap()[n, j])
            si_u, uslot, u_rdy = gu_ring.load(
                W["gup"].ap()[n, j + 8])
            mmg = mmu = None
            for hh in range(2):
                for k in range(DC):
                    for nt in range(2):
                        col = hh * 1024 + nt * 512
                        mmg = P.emit("tensor", lambda e,
                                     o=psA2.ap()[:, col:col + 512],
                                     l=gslot.ap()[:, k * 128:(k + 1) * 128],
                                     r=scaled.ap()[:, k * T + col: k * T + col + 512],
                                     s_=(k == 0):
                                     e.matmul(o, l, r, start=s_, stop=False, skip_group_check=True),
                                     waits=[g_rdy, t_sc, ev_g[hh]],
                                     inc=True if (k == DC - 1 and nt == 1) else None)
                for k in range(DC):
                    for nt in range(2):
                        col = hh * 1024 + nt * 512
                        mmu = P.emit("tensor", lambda e,
                                     o=psB2.ap()[:, col:col + 512],
                                     l=uslot.ap()[:, k * 128:(k + 1) * 128],
                                     r=scaled.ap()[:, k * T + col: k * T + col + 512],
                                     s_=(k == 0):
                                     e.matmul(o, l, r, start=s_, stop=False, skip_group_check=True),
                                     waits=[u_rdy, t_sc, ev_u[hh]],
                                     inc=True if (k == DC - 1 and nt == 1) else None)
                t1 = P.emit("scalar", lambda e,
                            o=g_scr2.ap()[:, hh * 1024:(hh + 1) * 1024],
                            i=psA2.ap()[:, hh * 1024:(hh + 1) * 1024]:
                            e.activation(o, i, AF.Silu), waits=[mmg, ev_u[hh]])
                ev_g[hh] = t1
                ev_u[hh] = P.emit("vector", lambda e,
                                  o=actb.ap()[:, j * T + hh * 1024: j * T + (hh + 1) * 1024],
                                  a=g_scr2.ap()[:, hh * 1024:(hh + 1) * 1024],
                                  b=psB2.ap()[:, hh * 1024:(hh + 1) * 1024]:
                                  e.tensor_tensor(o, a, b, ALU.mult),
                                  waits=[t1, mmu])
            gu_ring.free(si_g, mmg)
            gu_ring.free(si_u, mmu)
        t_act = ("multi", [ev_g[0], ev_g[1], ev_u[0], ev_u[1]])
        ev_d = [t_act, t_act]
        for mp in range(4):
            si_d, dslot, d_rdy = dn_ring.load(
                W["down"].ap()[n, 2 * mp:2 * mp + 2].rearrange("g p n -> p g n"),
                out_view=lambda a: a.rearrange("p (g n) -> p g n", g=2))
            mm_last = None
            for sub in range(2):
                mc = 2 * mp + sub
                pb = mc % 2
                pp = psA2 if pb == 0 else psB2
                mm = None
                for k in range(DC):
                    for nt in range(NT):
                        mm = P.emit("tensor", lambda e,
                                    o=pp.ap()[:, nt * 512:(nt + 1) * 512],
                                    l=dslot.ap()[:, sub * 1024 + k * 128: sub * 1024 + (k + 1) * 128],
                                    r=actb.ap()[:, k * T + nt * 512: k * T + (nt + 1) * 512],
                                    s_=(k == 0):
                                    e.matmul(o, l, r, start=s_, stop=False, skip_group_check=True),
                                    waits=[d_rdy, t_act, ev_d[pb]],
                                    inc=True if (k == DC - 1 and nt == NT - 1) else None)
                mm_last = mm
                ev_d[pb] = P.emit("vector", lambda e,
                                  o=xT.ap()[:, mc * T:(mc + 1) * T],
                                  a=xT.ap()[:, mc * T:(mc + 1) * T],
                                  i=pp.ap():
                                  e.tensor_tensor(o, a, i, ALU.add),
                                  waits=[mm, x_tok])
            dn_ring.free(si_d, mm_last)
        prev_done = ("multi", [ev_d[0], ev_d[1]])
    sc3.close()
    scsd.close()
    sch.close()
    P.new_phase()
    return P.tok("vector")


def decl_l1ffn_weights(nc, pfx="l1f"):
    dt = lambda n, s: nc.dram_tensor(pfx + "_" + n, s, F32, kind="ExternalInput")
    W = {}
    W["router"] = dt("router", [DC, 128, 16])
    W["gup"] = dt("gup", [16, 16, 128, DC * 128])
    W["down"] = dt("down", [16, DC, 128, DC * 128])
    W["sgate"] = dt("sgate", [32, 128, DC * 128])
    W["sup"] = dt("sup", [32, 128, DC * 128])
    W["sdown"] = dt("sdown", [DC, 128, 32 * 128])
    return W


def prep_l1ffn_weights(i, pfx="l1f"):
    m = {}
    rwT = np.ascontiguousarray(np.asarray(i["L1_ffn_router_w"], np.float32).T)
    m[pfx + "_router"] = np.ascontiguousarray(rwT.reshape(DC, 128, 16))
    m[pfx + "_gup"] = np.stack([chunk_w(np.asarray(i["L1_ffn_gate_up_proj"])[n], D, 2048)
                                for n in range(16)])
    m[pfx + "_down"] = np.stack([chunk_w(np.asarray(i["L1_ffn_down_proj"])[n], D, D)
                                 for n in range(16)])
    sg = np.ascontiguousarray(np.asarray(i["L1_ffn_shared_gate_w"], np.float32).T)
    su = np.ascontiguousarray(np.asarray(i["L1_ffn_shared_up_w"], np.float32).T)
    sd = np.ascontiguousarray(np.asarray(i["L1_ffn_shared_down_w"], np.float32).T)
    m[pfx + "_sgate"] = chunk_w(sg, D, 4096)
    m[pfx + "_sup"] = chunk_w(su, D, 4096)
    m[pfx + "_sdown"] = chunk_w(sd, 4096, D)
    return m



BLOCKS = {
    "l0a": (decl_l0attn_weights, bld_l0attn, prep_l0attn_weights, L0A_SMALL),
    "l0f": (decl_l0ffn_weights, bld_l0ffn, prep_l0ffn_weights, L0F_SMALL),
    "l1a": (decl_l1attn_weights, bld_l1attn, prep_l1attn_weights, []),
    "l1f": (decl_l1ffn_weights, bld_l1ffn, prep_l1ffn_weights, []),
}

_prog_cache = {}


def build_block_program(which):
    if which in _prog_cache:
        return _prog_cache[which]
    decl, bld, _, smalls = BLOCKS[which]
    nc = bass.Bass("TRN2", target_bir_lowering=False)
    ctx = contextlib.ExitStack()
    nc._kb_ctx = ctx
    xin = nc.dram_tensor("xin", [D, T], F32, kind="ExternalInput").ap()
    xout = nc.dram_tensor("xout", [D, T], F32, kind="ExternalOutput").ap()
    Wd = decl(nc)

    P = Prog(nc, ctx)
    consts = load_consts(P, nc, CONST_SPECS)
    pctr = P.new_ctr("params")
    if smalls:
        sbufize_params(P, Wd, smalls, pctr)

    xT = P.sbuf("xT", [128, DC * T], F32)
    ld = P.new_ctr("xld")
    P.emit("sync", lambda e: e.dma_start(
        out=xT.ap().rearrange("p (c t) -> p c t", c=DC),
        in_=xin.rearrange("(c p) t -> p c t", p=128)),
        inc=(ld, 16), nofence=True)
    P.fence_waits = [(ld, 16), consts["_tok"]]
    if pctr.val:
        P.fence_waits.append((pctr, pctr.val))

    if which == "l0a":
        xres_dram = nc.dram_tensor("xres_scr", [D, T], F32, kind="Internal").ap()
        bld(P, xT, consts, Wd, xres_dram, x_tok=None)
    else:
        bld(P, xT, consts, Wd, x_tok=None)

    st = P.new_ctr("xst")
    P.emit("sync", lambda e: e.dma_start(
        out=xout.rearrange("(c p) t -> p c t", p=128),
        in_=xT.ap().rearrange("p (c t) -> p c t", c=DC)),
        inc=(st, 16))
    fin = [(st, 16)]
    if hasattr(P, "dbg_ctr"):
        fin.append((P.dbg_ctr, P.dbg_ctr.val))
    P.emit("sync", lambda e: e.nop(), waits=fin)

    block = ctx.enter_context(nc.Block())
    P.emit_all(block)
    ctx.close()
    _prog_cache[which] = nc
    return nc


def block_in_map(which, inputs, xT_core):
    _, _, prep, _ = BLOCKS[which]
    m = prep(inputs)
    m["xin"] = np.ascontiguousarray(xT_core)
    m.update(const_in_map())
    return m


def kernel(**inputs):
    import os
    if os.environ.get("KB_UNFUSED"):
        return kernel_unfused(**inputs)
    return kernel_fused(**inputs)



_fused_cache = {}


def build_fused_program():
    if "nc" in _fused_cache:
        return _fused_cache["nc"]
    nc = bass.Bass("TRN2", target_bir_lowering=False)
    ctx = contextlib.ExitStack()
    nc._kb_ctx = ctx
    xin = nc.dram_tensor("xin", [D, T], F32, kind="ExternalInput").ap()
    xout = nc.dram_tensor("xout", [D, T], F32, kind="ExternalOutput").ap()
    xres_dram = nc.dram_tensor("xres_scr", [D, T], F32, kind="Internal").ap()
    Wa = decl_l0attn_weights(nc)
    Wf = decl_l0ffn_weights(nc)
    W1a = decl_l1attn_weights(nc)
    W1f = decl_l1ffn_weights(nc)

    P = Prog(nc, ctx)
    consts = load_consts(P, nc, CONST_SPECS)
    pctr = P.new_ctr("params")
    sca_par = P.scope()
    sbufize_params(P, Wa, L0A_SMALL, pctr, scope=sca_par)
    sbufize_params(P, Wf, L0F_SMALL, pctr)

    xT = P.sbuf("xT", [128, DC * T], F32)
    ld = P.new_ctr("xld")
    P.emit("sync", lambda e: e.dma_start(
        out=xT.ap().rearrange("p (c t) -> p c t", c=DC),
        in_=xin.rearrange("(c p) t -> p c t", p=128)),
        inc=(ld, 16), nofence=True)
    P.fence_waits = [(ld, 16), consts["_tok"], (pctr, pctr.val)]

    import os as _os
    nb = int(_os.environ.get("KB_FUSE_BLOCKS", "4"))
    bld_l0attn(P, xT, consts, Wa, xres_dram, x_tok=None)
    sca_par.close()
    if nb >= 2:
        bld_l0ffn(P, xT, consts, Wf, x_tok=None)
    if nb >= 3:
        bld_l1attn(P, xT, consts, W1a, x_tok=None)
    if nb >= 4:
        bld_l1ffn(P, xT, consts, W1f, x_tok=None)

    st = P.new_ctr("xst")
    P.emit("sync", lambda e: e.dma_start(
        out=xout.rearrange("(c p) t -> p c t", p=128),
        in_=xT.ap().rearrange("p (c t) -> p c t", c=DC)),
        inc=(st, 16))
    fin = [(st, 16)]
    if hasattr(P, "dbg_ctr"):
        fin.append((P.dbg_ctr, P.dbg_ctr.val))
    P.emit("sync", lambda e: e.nop(), waits=fin)

    block = ctx.enter_context(nc.Block())
    P.emit_all(block)
    ctx.close()
    _fused_cache["nc"] = nc
    return nc


def fused_in_map(inputs, x_core):
    m = {}
    m.update(prep_l0attn_weights(inputs))
    m.update(prep_l0ffn_weights(inputs))
    m.update(prep_l1attn_weights(inputs))
    m.update(prep_l1ffn_weights(inputs))
    m["xin"] = np.ascontiguousarray(np.asarray(x_core, dtype=np.float32).T)
    m.update(const_in_map())
    return m


def kernel_fused(**inputs):
    x = np.asarray(inputs["L0_attn_inputs"], dtype=np.float32)
    nc = build_fused_program()
    in_maps = [fused_in_map(inputs, x[c]) for c in range(NCORES)]
    res = run_bass_kernel_spmd(nc, in_maps, core_ids=list(range(NCORES)))
    out = np.stack([r["xout"].T for r in res.results], axis=0)
    return np.ascontiguousarray(out.astype(np.float32))


def kernel_unfused(**inputs):
    x = np.asarray(inputs["L0_attn_inputs"], dtype=np.float32)
    xt = [np.ascontiguousarray(x[c].T) for c in range(NCORES)]
    for which in ("l0a", "l0f", "l1a", "l1f"):
        nc = build_block_program(which)
        in_maps = [block_in_map(which, inputs, xt[c]) for c in range(NCORES)]
        res = run_bass_kernel_spmd(nc, in_maps, core_ids=list(range(NCORES)))
        xt = [r["xout"] for r in res.results]
    out = np.stack([t.T for t in xt], axis=0)
    return np.ascontiguousarray(out.astype(np.float32))
